# version 08
# speedup vs baseline: 1.6336x; 1.0076x over previous
import math

import numpy as np
import concourse.bass as bass
import concourse.mybir as mybir
from concourse.bass_utils import run_bass_kernel_spmd

F32 = mybir.dt.float32
BF16 = mybir.dt.bfloat16
I32 = mybir.dt.int32
AF = mybir.ActivationFunctionType
ALU = mybir.AluOpType
AX = mybir.AxisListType

B, S, E, H, HD = 2, 4096, 2048, 16, 128
NB = E // 128
TC = 1024
SQD = 1.0 / math.sqrt(HD)
SQDI = math.sqrt(HD)
MOE_E, MOE_F, MOE_FS = 8, 1408, 2816
NFC = MOE_F // 128
I_FFN = 8192
LN_EPS = 1e-5
NS = 8
SHUF = [i + 1 if i % 2 == 0 else i - 1 for i in range(32)]

_TAPS = ()
_UPTO = 99


def _core_rows(cc):
    idx = []
    for j in range(8):
        g = 4 * j + cc
        idx.extend(range(g * 128, (g + 1) * 128))
    return np.array(idx)



def _packT(wT, fw):
    K, F = wT.shape
    nf = F // fw
    a = wT.reshape(K // 128, 128, nf, fw)
    return np.ascontiguousarray(a.transpose(2, 1, 0, 3).reshape(nf, 128, (K // 128) * fw))


def _prep_inputs(inp):
    f32 = np.float32
    c = np.ascontiguousarray

    half = np.arange(0, HD, 2, dtype=np.float64) / HD
    theta = np.power(10000.0, -np.repeat(half, 2))
    sign = np.where(np.arange(HD) % 2 == 0, -1.0, 1.0)

    per_core = []
    for cid in range(8):
        b, cc = cid // 4, cid % 4
        rows = _core_rows(cc)
        d = {}
        d["x_sh"] = c(np.asarray(inp["L0_attn_x"])[b][rows].T.astype(f32)
                      .reshape(NB, 128, TC).transpose(1, 0, 2).reshape(128, NB * TC))
        ang = rows.astype(np.float64)[None, :] * theta[:, None]
        d["rope_cos"] = c((np.cos(ang) * SQD).astype(f32))
        d["rope_sin"] = c((np.sin(ang) * SQD * sign[:, None]).astype(f32))
        d["rope_cos_k"] = c(np.cos(ang).astype(f32))
        d["rope_sin_k"] = c((np.sin(ang) * sign[:, None]).astype(f32))
        msk = np.zeros((128, 512), dtype=f32)
        for m in range(4):
            if m < cc:
                msk[:, m * 128:(m + 1) * 128] = 1.0
            elif m == cc:
                msk[:, m * 128:(m + 1) * 128] = np.tril(np.ones((128, 128), f32)).T
        d["mask"] = c(msk)
        per_core.append(d)

    sh = {}
    for nm in ("Wq", "Wk", "Wo"):
        sh[f"l0_{nm.lower()}T"] = _packT(np.asarray(inp[f"L0_attn_{nm}"]).T.astype(f32), 128)
        sh[f"l1_{nm.lower()}T"] = _packT(np.asarray(inp[f"L1_attn_{nm}"]).T.astype(f32), 128)
    sh["l0_wvT"] = _packT(np.asarray(inp["L0_attn_Wv"]).T.astype(f32), 256)
    sh["l1_wvT"] = _packT(np.asarray(inp["L1_attn_Wv"]).T.astype(f32), 256)
    sh["l0_bq"] = c(np.asarray(inp["L0_attn_bq"]).reshape(1, E).astype(f32))
    sh["l0_bk"] = c(np.asarray(inp["L0_attn_bk"]).reshape(1, E).astype(f32))
    sh["l0_bv_row"] = c(np.asarray(inp["L0_attn_bv"]).reshape(1, E).astype(f32))
    sh["l0_bo_row"] = c(np.asarray(inp["L0_attn_bo"]).reshape(1, E).astype(f32))
    sh["l0_wrT"] = c(np.asarray(inp["L0_ffn_Wr"]).T.astype(f32)
                     .reshape(NB, 128, MOE_E).transpose(1, 0, 2).reshape(128, NB * MOE_E))
    sh["l0_wsgT"] = _packT(np.asarray(inp["L0_ffn_Wsg"]).T.astype(f32), 128)
    sh["l0_wsuT"] = _packT(np.asarray(inp["L0_ffn_Wsu"]).T.astype(f32), 128)
    sh["l0_wsdT"] = _packT(np.asarray(inp["L0_ffn_Wsd"]).T.astype(f32), 128)
    sh["l0_wgT"] = c(np.stack([_packT(np.asarray(inp["L0_ffn_Wg"])[e].T.astype(f32), 128)
                               for e in range(MOE_E)]))
    sh["l0_wuT"] = c(np.stack([_packT(np.asarray(inp["L0_ffn_Wu"])[e].T.astype(f32), 128)
                               for e in range(MOE_E)]))
    sh["l0_wdT"] = c(np.stack([_packT(np.asarray(inp["L0_ffn_Wd"])[e].astype(f32).T, 128)
                               for e in range(MOE_E)]))
    sp = np.asarray(inp["L1_attn_splat_positions"])
    sd = np.asarray(inp["L1_attn_splat_directions"])
    sh["splat_posT"] = c(sp.reshape(H * NS, HD).T.astype(f32))
    sh["splat_dirT"] = c(sd.reshape(H * NS, HD).T.astype(f32))
    sh["splat_lsc"] = c(np.asarray(inp["L1_attn_splat_log_scales"]).reshape(1, H * NS).astype(f32))
    sh["splat_lam"] = c(np.asarray(inp["L1_attn_splat_log_amplitudes"]).reshape(1, H * NS).astype(f32))
    sh["c_ds"] = c(np.asarray(inp["L1_attn_directional_strength"], f32).reshape(1, 1))
    sh["c_ps"] = c(np.asarray(inp["L1_attn_position_sensitivity"], f32).reshape(1, 1))
    sh["c_gsa"] = c(np.asarray(inp["L1_attn_gsa_strength"], f32).reshape(1, 1))
    sh["l1_gwT"] = _packT(np.asarray(inp["L1_ffn_gate_w"]).T, 128)
    sh["l1_uwT"] = _packT(np.asarray(inp["L1_ffn_up_w"]).T, 128)
    sh["l1_dwT"] = _packT(np.asarray(inp["L1_ffn_down_w"]).T, 128)
    sh["l1_gsc"] = c(np.asarray(inp["L1_ffn_gate_scale"]).reshape(64, 128).T.astype(f32))
    sh["l1_usc"] = c(np.asarray(inp["L1_ffn_up_scale"]).reshape(64, 128).T.astype(f32))
    sh["l1_dsc"] = c(np.asarray(inp["L1_ffn_down_scale"]).reshape(16, 128).T.astype(f32))

    return [dict(sh, **pc) for pc in per_core]



class _Op:
    __slots__ = ("eng", "fn", "after", "amt", "cum", "needed")


class Sched:
    ENGS = ("sync", "gpsimd", "tensor", "vector", "scalar")

    def __init__(self, nc):
        self.nc = nc
        self.sem = {e: nc.alloc_semaphore("S_" + e) for e in self.ENGS}
        self.cum = {e: 0 for e in self.ENGS}
        self.waited = {e: {e2: 0 for e2 in self.ENGS} for e in self.ENGS}
        self.cur = []

    def add(self, eng, fn, after=(), dma=False, keep=False, strict=()):
        op = _Op()
        op.eng = eng
        op.fn = fn
        op.after = []
        for a in after or ():
            if a is None or a.eng == eng:
                continue
            if a.cum is None:
                assert a in self.cur, "dependency must precede the dependent op"
                a.needed = True
            op.after.append(a)
        for a in strict or ():
            if a is None:
                continue
            if a.cum is None:
                assert a in self.cur, "dependency must precede the dependent op"
                a.needed = True
            op.after.append(a)
        op.amt = 16 if dma else 1
        op.cum = None
        op.needed = keep
        self.cur.append(op)
        return op

    def flush(self):
        ops, self.cur = self.cur, []
        for op in ops:
            if op.needed:
                self.cum[op.eng] += op.amt
                op.cum = self.cum[op.eng]
        by = {e: [o for o in ops if o.eng == e] for e in self.ENGS}
        with self.nc.Block() as blk:
            for e in self.ENGS:
                if not by[e]:
                    continue

                def mk(e, ops_e):
                    def emitter(engobj):
                        for op in ops_e:
                            for a in op.after:
                                assert a.cum is not None, "dep op was never kept"
                                if self.waited[e][a.eng] < a.cum:
                                    engobj.wait_ge(self.sem[a.eng], a.cum)
                                    self.waited[e][a.eng] = a.cum
                            ins = op.fn(engobj)
                            if op.needed:
                                ins.then_inc(self.sem[op.eng], op.amt)
                    return emitter

                getattr(blk, e)(mk(e, by[e]))


def _mm_acc(te, bank, pairs, tail=None):
    last = None
    n = len(pairs)
    for i, (l, r) in enumerate(pairs):
        last = te.matmul(bank, l, r, start=(i == 0),
                         stop=(tail is None and i == n - 1))
    if tail is not None:
        last = te.matmul(bank, tail[0], tail[1], start=False, stop=True)
    return last


class KCtx:
    def __init__(self, nc):
        self.nc = nc
        self.sch = Sched(nc)
        self.t = {}
        self.ops = {}
        self.dbg_on = False

    def dbg(self, name, ap, shape, after):
        if not self.dbg_on or ap is None:
            return
        dt_ = nc_dbg_dtype(ap)
        d = self.nc.dram_tensor("dbg_" + name, list(shape), dt_, kind="ExternalOutput")
        o = self.sch.add("sync", lambda s: s.dma_start(out=d.ap(), in_=ap),
                         after=after, dma=True, keep=True)
        self.sch.add("vector", lambda v: v.engine_nop(), after=[o], keep=True)


def _din(nc, name, shape, dtype=F32):
    return nc.dram_tensor(name, list(shape), dtype, kind="ExternalInput")


def nc_dbg_dtype(ap):
    try:
        return ap.tensor.dtype
    except AttributeError:
        return ap.dtype



def build_nc(taps=(), upto=99):
    nc = bass.Bass("TRN2", target_bir_lowering=False, detect_race_conditions=False)
    C = KCtx(nc)
    t = C.t

    t["x_sh"] = _din(nc, "x_sh", (128, NB * TC))
    for nm in ("rope_cos", "rope_sin", "rope_cos_k", "rope_sin_k"):
        t[nm] = _din(nc, nm, (HD, TC))
    t["mask"] = _din(nc, "mask", (128, 512))
    for p in ("l0", "l1"):
        for nm in ("wq", "wk", "wo"):
            t[f"{p}_{nm}T"] = _din(nc, f"{p}_{nm}T", (NB, 128, E))
        t[f"{p}_wvT"] = _din(nc, f"{p}_wvT", (8, 128, NB * 256))
    t["l0_bq"] = _din(nc, "l0_bq", (1, E))
    t["l0_bk"] = _din(nc, "l0_bk", (1, E))
    t["l0_bv_row"] = _din(nc, "l0_bv_row", (1, E))
    t["l0_bo_row"] = _din(nc, "l0_bo_row", (1, E))
    t["l0_wrT"] = _din(nc, "l0_wrT", (128, NB * MOE_E))
    t["l0_wsgT"] = _din(nc, "l0_wsgT", (MOE_FS // 128, 128, E))
    t["l0_wsuT"] = _din(nc, "l0_wsuT", (MOE_FS // 128, 128, E))
    t["l0_wsdT"] = _din(nc, "l0_wsdT", (NB, 128, MOE_FS))
    t["l0_wgT"] = _din(nc, "l0_wgT", (MOE_E, NFC, 128, E))
    t["l0_wuT"] = _din(nc, "l0_wuT", (MOE_E, NFC, 128, E))
    t["l0_wdT"] = _din(nc, "l0_wdT", (MOE_E, NB, 128, MOE_F))
    t["splat_posT"] = _din(nc, "splat_posT", (HD, H * NS))
    t["splat_dirT"] = _din(nc, "splat_dirT", (HD, H * NS))
    t["splat_lsc"] = _din(nc, "splat_lsc", (1, H * NS))
    t["splat_lam"] = _din(nc, "splat_lam", (1, H * NS))
    t["c_ds"] = _din(nc, "c_ds", (1, 1))
    t["c_ps"] = _din(nc, "c_ps", (1, 1))
    t["c_gsa"] = _din(nc, "c_gsa", (1, 1))
    t["l1_gwT"] = _din(nc, "l1_gwT", (I_FFN // 128, 128, E), I32)
    t["l1_uwT"] = _din(nc, "l1_uwT", (I_FFN // 128, 128, E), I32)
    t["l1_dwT"] = _din(nc, "l1_dwT", (NB, 128, I_FFN), I32)
    t["l1_gsc"] = _din(nc, "l1_gsc", (128, 64))
    t["l1_usc"] = _din(nc, "l1_usc", (128, 64))
    t["l1_dsc"] = _din(nc, "l1_dsc", (128, 16))

    t["y"] = nc.dram_tensor("y", [128, NB * TC], F32, kind="ExternalOutput")
    for tp in taps:
        t["tap_" + tp] = nc.dram_tensor("tap_" + tp, [E, TC], F32, kind="ExternalOutput")

    t["k_in"] = nc.dram_tensor("k_in", [H, HD, TC], BF16, kind="Internal")
    t["v_in"] = nc.dram_tensor("v_in", [H, TC, HD], BF16, kind="Internal")
    t["k_ga"] = nc.dram_tensor("k_ga", [H, 4, HD, TC], BF16, kind="Internal")
    t["v_ga"] = nc.dram_tensor("v_ga", [H, 4, TC, HD], BF16, kind="Internal")

    t["X"] = nc.alloc_sbuf_tensor("X", [128, NB * TC], F32)
    t["Xb"] = nc.alloc_sbuf_tensor("Xb", [128, NB * TC], BF16)
    t["Qh"] = nc.alloc_sbuf_tensor("Qh", [128, H * TC], BF16)
    t["_aQh"] = nc.lookup_mloc(t["Qh"]).addr
    t["ones_row"] = nc.alloc_sbuf_tensor("ones_row", [33, 512], F32)
    t["onesb"] = nc.alloc_sbuf_tensor("onesb", [33, 512], BF16)
    t["ones_col_bf"] = nc.alloc_sbuf_tensor("ones_col_bf", [128, 1], BF16)
    t["ones_col_f32"] = nc.alloc_sbuf_tensor("ones_col_f32", [128, 1], F32)
    t["mean_col_bf"] = nc.alloc_sbuf_tensor("mean_col_bf", [128, 1], BF16)
    t["neg128_row"] = nc.alloc_sbuf_tensor("neg128_row", [1, 128], BF16)
    t["mask_sb"] = nc.alloc_sbuf_tensor("mask_sb", [128, 512], BF16)
    t["ident32"] = nc.alloc_sbuf_tensor("ident32", [128, 128], F32)
    t["selD"] = nc.alloc_sbuf_tensor("selD", [128, 66], BF16)
    sc16 = nc.alloc_sbuf_tensor("sc16", [128, 16384], mybir.dt.uint8)
    _a16 = nc.lookup_mloc(sc16).addr
    for i, nm in enumerate(("cosq", "sinq", "cosk", "sink")):
        t[nm] = nc.alloc_sbuf_tensor_at(nm, [128, TC], F32, offset=_a16 + i * 4096)
    t["_a16"] = _a16

    C.dbg_on = "dbg" in taps
    C.router_only = "routeronly" in taps
    stage = [0]

    def go():
        stage[0] += 1
        return stage[0] <= upto

    for layer in (0, 1):
        if layer == 0:
            if go():
                _p0_load(C)
        if go():
            _q_proj(C, layer)
        if go():
            _k_proj(C, layer)
        if go():
            _v_proj(C, layer)
        if go():
            _kv_gather(C, layer)
        if go():
            if layer == 1:
                _splat_setup(C)
            _attn_heads(C, layer)
        if go():
            _out_proj(C, layer)
        if go():
            _ln(C)
        if layer == 0:
            if "l0a" in taps:
                _tap(C, "l0a")
            if go():
                _moe_ffn(C)
            if go():
                _ln(C)
            if "l0f" in taps:
                _tap(C, "l0f")
        else:
            if "l1a" in taps:
                _tap(C, "l1a")
            if go():
                _quant_ffn(C)
            if go():
                _ln(C)
    _store_out(C)
    return nc



def _p0_load(C):
    nc, t, sch = C.nc, C.t, C.sch

    oxc = {}
    for c in range(8):
        oxc[c] = sch.add("sync" if c % 2 == 0 else "gpsimd", lambda s, c=c: s.dma_start(
            out=t["X"][:, c * 2 * TC:(c + 1) * 2 * TC],
            in_=t["x_sh"].ap()[:, c * 2 * TC:(c + 1) * 2 * TC]), dma=True, keep=True)
    for nm, dst in (("rope_cos", "cosq"), ("rope_sin", "sinq"),
                    ("rope_cos_k", "cosk"), ("rope_sin_k", "sink"),
                    ("mask", "mask_sb")):
        C.ops["ld_" + dst] = sch.add(
            "gpsimd", lambda g, nm=nm, dst=dst: g.dma_start(out=t[dst][:, :], in_=t[nm].ap()),
            dma=True, keep=True)
    sch.add("gpsimd", lambda g: g.memset(t["ones_row"][:, :], 1.0), keep=True)
    sch.add("gpsimd", lambda g: g.memset(t["onesb"][:, :], 1.0), keep=True)
    sch.add("gpsimd", lambda g: g.memset(t["ones_col_bf"][:, :], 1.0), keep=True)
    sch.add("gpsimd", lambda g: g.memset(t["ones_col_f32"][:, :], 1.0), keep=True)
    sch.add("gpsimd", lambda g: g.memset(t["mean_col_bf"][:, :], 1.0 / E), keep=True)
    sch.add("gpsimd", lambda g: g.memset(t["neg128_row"][:, :], -128.0), keep=True)
    sch.add("gpsimd", lambda g: g.memset(t["selD"][:, :], 0.0), keep=True)
    sch.add("gpsimd", lambda g: g.memset(t["selD"][:, 0:1], 1.0), keep=True)
    C.ops["selD_ready"] = sch.add("gpsimd", lambda g: g.memset(t["selD"][:, 65:66], 1.0),
                                  keep=True)
    C.ops["consts"] = C.ops["selD_ready"]

    with nc.sbuf_tensor("iota_r", [128, 128], F32) as ir, \
            nc.sbuf_tensor("iota_c", [128, 128], F32) as ic:
        o1 = sch.add("gpsimd", lambda g: g.iota(ir[:, :], [[0, 128]], channel_multiplier=1,
                                                allow_small_or_imprecise_dtypes=True), keep=True)
        o2 = sch.add("gpsimd", lambda g: g.iota(ic[:, :], [[1, 128]], channel_multiplier=0,
                                                allow_small_or_imprecise_dtypes=True), keep=True)
        C.ops["ident_ready"] = sch.add(
            "vector", lambda v: v.tensor_tensor(t["ident32"][:, :], ir[:, :], ic[:, :],
                                                ALU.is_equal), after=[o1, o2], keep=True)

        o = None
        for ec in range(NB):
            o = sch.add("vector", lambda v, ec=ec: v.tensor_copy(
                t["Xb"][:, ec * TC:(ec + 1) * TC], t["X"][:, ec * TC:(ec + 1) * TC]),
                after=[oxc[ec // 2]], keep=(ec == NB - 1))
        C.ops["xb_ready"] = o
        sch.flush()



def _q_proj(C, layer):
    nc, t, sch = C.nc, C.t, C.sch
    w = t["l0_wqT"] if layer == 0 else t["l1_wqT"]
    mm = {}
    ev = {}
    rp = {}

    with nc.sbuf_tensor(f"wrq{layer}", [128, 2 * NB * 128], BF16) as wring, \
            nc.sbuf_tensor(f"qraw{layer}", [128, 2 * TC], F32) as raw, \
            nc.sbuf_tensor(f"qshf{layer}", [128, TC], F32) as shuf, \
            nc.sbuf_tensor(f"bqr{layer}", [1, E], BF16) as bqr, \
            nc.psum_tensor(f"ppq{layer}", [128, 4 * 512], F32) as pp:
        obq = sch.add("gpsimd", lambda s: s.dma_start(out=bqr[:1, :], in_=t["l0_bq"].ap()),
                      dma=True, keep=True) if layer == 0 else None
        for fc in range(NB):
            slot = fc % 2
            ld = sch.add("gpsimd", lambda g, fc=fc, slot=slot: g.dma_start(
                out=wring[:, slot * NB * 128:(slot + 1) * NB * 128],
                in_=w.ap()[fc]),
                after=[mm.get((fc - 2, 1))], dma=True, keep=True)
            for tt in range(2):
                bank = pp[:, (slot * 2 + tt) * 512:(slot * 2 + tt + 1) * 512]

                def mmfn(te, bank=bank, slot=slot, tt=tt, fc=fc, layer=layer):
                    pairs = [(wring[:, slot * NB * 128 + ec * 128: slot * NB * 128 + (ec + 1) * 128],
                              t["Xb"][:, ec * TC + tt * 512: ec * TC + (tt + 1) * 512])
                             for ec in range(NB)]
                    tail = ((bqr[:1, fc * 128:(fc + 1) * 128],
                             t["onesb"][:1, :512]) if layer == 0 else None)
                    return _mm_acc(te, bank, pairs, tail=tail)

                mm[(fc, tt)] = sch.add("tensor", mmfn,
                                       after=[ld, ev.get((fc - 2, tt)), C.ops["xb_ready"],
                                              C.ops["consts"],
                                              obq], keep=True)
                if layer == 0:
                    ev[(fc, tt)] = sch.add("scalar", lambda sc, bank=bank, fc=fc, tt=tt:
                                           sc.activation(
                        raw[:, (fc % 2) * TC + tt * 512:(fc % 2) * TC + tt * 512 + 512],
                        bank, AF.Copy),
                        after=[mm[(fc, tt)], rp.get(fc - 2)], keep=True)
                else:
                    ev[(fc, tt)] = sch.add("scalar", lambda sc, bank=bank, fc=fc, tt=tt:
                                           sc.activation(
                        t["Qh"][:, fc * TC + tt * 512: fc * TC + tt * 512 + 512],
                        bank, AF.Copy, scale=SQD),
                        after=[mm[(fc, tt)]], keep=True)
            if layer == 0:
                def ropefn(v, fc=fc, slot=slot):
                    src = raw[:, slot * TC: slot * TC + TC]
                    dst = t["Qh"][:, fc * TC:(fc + 1) * TC]
                    v.stream_shuffle(shuf[:, :], src, SHUF)
                    v.tensor_mul(shuf[:, :], shuf[:, :], t["sinq"][:, :])
                    v.tensor_mul(src, src, t["cosq"][:, :])
                    return v.tensor_add(dst, src, shuf[:, :])
                rp[fc] = sch.add("vector", ropefn,
                                 after=[ev[(fc, 0)], ev[(fc, 1)],
                                        C.ops["ld_cosq"], C.ops["ld_sinq"]], keep=True)
        sch.flush()
    C.ops["qdone"] = rp[NB - 1] if layer == 0 else ev[(NB - 1, 1)]


def _k_proj(C, layer):
    nc, t, sch = C.nc, C.t, C.sch
    w = t["l0_wkT"] if layer == 0 else t["l1_wkT"]
    mm = {}
    ev = {}
    rp = {}
    st = {}

    with nc.sbuf_tensor(f"wrk{layer}", [128, 2 * NB * 128], BF16) as wring, \
            nc.sbuf_tensor(f"kraw{layer}", [128, 2 * TC], F32) as raw, \
            nc.sbuf_tensor(f"kshf{layer}", [128, TC], F32) as shuf, \
            nc.sbuf_tensor(f"krot{layer}", [128, 2 * TC], BF16) as rot, \
            nc.sbuf_tensor(f"bkr{layer}", [1, E], BF16) as bkr, \
            nc.psum_tensor(f"ppk{layer}", [128, 4 * 512], F32) as pp:
        obk = sch.add("gpsimd", lambda s: s.dma_start(out=bkr[:1, :], in_=t["l0_bk"].ap()),
                      dma=True, keep=True) if layer == 0 else None
        for fc in range(NB):
            slot = fc % 2
            ld = sch.add("gpsimd", lambda g, fc=fc, slot=slot: g.dma_start(
                out=wring[:, slot * NB * 128:(slot + 1) * NB * 128],
                in_=w.ap()[fc]),
                after=[mm.get((fc - 2, 1))], dma=True, keep=True)
            for tt in range(2):
                bank = pp[:, (slot * 2 + tt) * 512:(slot * 2 + tt + 1) * 512]

                def mmfn(te, bank=bank, slot=slot, tt=tt, fc=fc, layer=layer):
                    pairs = [(wring[:, slot * NB * 128 + ec * 128: slot * NB * 128 + (ec + 1) * 128],
                              t["Xb"][:, ec * TC + tt * 512: ec * TC + (tt + 1) * 512])
                             for ec in range(NB)]
                    tail = ((bkr[:1, fc * 128:(fc + 1) * 128],
                             t["onesb"][:1, :512]) if layer == 0 else None)
                    return _mm_acc(te, bank, pairs, tail=tail)

                mm[(fc, tt)] = sch.add("tensor", mmfn,
                                       after=[ld, ev.get((fc - 2, tt)), C.ops["xb_ready"],
                                              C.ops["consts"],
                                              obk], keep=True)
                war = rp.get(fc - 2) if layer == 0 else st.get(fc - 2)
                evdst = raw if layer == 0 else rot
                ev[(fc, tt)] = sch.add("scalar", lambda sc, bank=bank, fc=fc, tt=tt,
                                       evdst=evdst: sc.activation(
                    evdst[:, (fc % 2) * TC + tt * 512:(fc % 2) * TC + tt * 512 + 512],
                    bank, AF.Copy),
                    after=[mm[(fc, tt)], war], keep=True)
            if layer == 0:
                def ropefn(v, fc=fc, slot=slot):
                    src = raw[:, slot * TC: slot * TC + TC]
                    dst = rot[:, slot * TC: slot * TC + TC]
                    v.stream_shuffle(shuf[:, :], src, SHUF)
                    v.tensor_mul(shuf[:, :], shuf[:, :], t["sink"][:, :])
                    v.tensor_mul(src, src, t["cosk"][:, :])
                    return v.tensor_add(dst, src, shuf[:, :])
                rp[fc] = sch.add("vector", ropefn,
                                 after=[ev[(fc, 0)], ev[(fc, 1)], st.get(fc - 2),
                                        C.ops["ld_cosk"], C.ops["ld_sink"]], keep=True)
                src_sb = rot
            else:
                rp[fc] = None
                src_sb = rot
            st[fc] = sch.add("sync", lambda s, fc=fc, slot=slot, src_sb=src_sb: s.dma_start(
                out=t["k_in"].ap()[fc, :, :],
                in_=src_sb[:, slot * TC: slot * TC + TC]),
                after=[rp[fc] or ev[(fc, 1)], ev[(fc, 0)], C.ops.get("ccK")],
                dma=True, keep=True)
            C.ops[f"kst{fc}"] = st[fc]
        sch.flush()
    C.ops["kst_last"] = st[NB - 1]


def _v_proj(C, layer):
    nc, t, sch = C.nc, C.t, C.sch
    w = t["l0_wvT"] if layer == 0 else t["l1_wvT"]
    mm = {}
    ev = {}
    st = {}

    with nc.sbuf_tensor(f"wrv{layer}", [128, 2 * NB * 256], BF16) as wring, \
            nc.sbuf_tensor(f"vt{layer}", [128, 2 * 256], BF16) as vtmp, \
            nc.sbuf_tensor(f"bvr{layer}", [1, E], BF16) as bvr, \
            nc.psum_tensor(f"ppv{layer}", [128, 4 * 512], F32) as pp:
        obv = sch.add("gpsimd", lambda s: s.dma_start(out=bvr[:1, :], in_=t["l0_bv_row"].ap()),
                      dma=True, keep=True) if layer == 0 else None
        n = 0
        for ft in range(8):
            ld = sch.add("gpsimd", lambda g, ft=ft: g.dma_start(
                out=wring[:, (ft % 2) * NB * 256:(ft % 2 + 1) * NB * 256],
                in_=w.ap()[ft]),
                after=[mm.get(8 * (ft - 2) + 7)], dma=True, keep=True)
            for tc in range(8):
                bank = pp[:, (n % 4) * 512:(n % 4) * 512 + 256]

                def mmfn(te, bank=bank, ft=ft, tc=tc):
                    pairs = [(t["Xb"][:, ec * TC + tc * 128: ec * TC + (tc + 1) * 128],
                              wring[:, (ft % 2) * NB * 256 + ec * 256:(ft % 2) * NB * 256 + (ec + 1) * 256])
                             for ec in range(NB)]
                    tail = ((t["onesb"][:1, :128], bvr[:1, ft * 256:(ft + 1) * 256])
                            if layer == 0 else None)
                    return _mm_acc(te, bank, pairs, tail=tail)

                dep_b = obv
                mm[n] = sch.add("tensor", mmfn,
                                after=[ld, ev.get(n - 4), C.ops["xb_ready"], dep_b,
                                       C.ops["consts"]], keep=True)
                ev[n] = sch.add("scalar", lambda sc, bank=bank, n=n: sc.activation(
                    vtmp[:, (n % 2) * 256:(n % 2 + 1) * 256], bank, AF.Copy),
                    after=[mm[n], st.get(n - 2)], keep=True)
                st[n] = sch.add("sync", lambda s, ft=ft, tc=tc, n=n: s.dma_start(
                    out=t["v_in"].ap()[2 * ft:2 * ft + 2, tc * 128:(tc + 1) * 128, :]
                    .rearrange("h t d -> t h d"),
                    in_=vtmp[:, (n % 2) * 256:(n % 2 + 1) * 256]
                    .rearrange("t (h d) -> t h d", h=2)),
                    after=[ev[n], C.ops.get("ccV")], dma=True, keep=True)
                if tc == 7:
                    C.ops[f"vst{2 * ft}"] = st[n]
                    C.ops[f"vst{2 * ft + 1}"] = st[n]
                n += 1
        sch.flush()
    C.ops["vst_last"] = st[63]


def _kv_gather(C, layer):
    pass



def _splat_setup(C):
    nc, t, sch = C.nc, C.t, C.sch
    t["spl"] = nc.alloc_sbuf_tensor("spl", [128, 4 * 128], BF16)
    t["spl_rows"] = nc.alloc_sbuf_tensor("spl_rows", [1, 3 * 128], F32)
    t["spl_rowsb"] = nc.alloc_sbuf_tensor("spl_rowsb", [1, 5 * 128], BF16)
    t["spl_cols"] = nc.alloc_sbuf_tensor("spl_cols", [128, 8], F32)

    rowsb = t["spl_rowsb"]
    t["cq_row"] = rowsb[:1, 0:128]
    t["ck_row"] = rowsb[:1, 128:256]
    t["coq_row"] = rowsb[:1, 256:384]
    t["cok_row"] = rowsb[:1, 384:512]
    t["pd_row"] = rowsb[:1, 512:640]
    rows = t["spl_rows"]
    sc_row = rows[:1, 0:128]
    am_row = rows[:1, 128:256]
    rn_row = rows[:1, 256:384]
    cols = t["spl_cols"]

    with nc.sbuf_tensor("sp_pos", [128, 128], F32) as pos, \
            nc.sbuf_tensor("sp_dir", [128, 128], F32) as dire, \
            nc.sbuf_tensor("sp_sq", [128, 128], F32) as sq, \
            nc.sbuf_tensor("sp_rb", [128, 128], F32) as rb, \
            nc.sbuf_tensor("sp_r1", [1, 128], F32) as r1, \
            nc.sbuf_tensor("sp_s1", [1, 9], F32) as s1, \
            nc.psum_tensor("pspl", [128, 2 * 512], F32) as pp:
        b0 = pp[:, 0:512]
        b1 = pp[:, 512:1024]

        lp = sch.add("sync", lambda s: s.dma_start(out=pos[:, :], in_=t["splat_posT"].ap()),
                     dma=True, keep=True)
        ld = sch.add("sync", lambda s: s.dma_start(out=dire[:, :], in_=t["splat_dirT"].ap()),
                     dma=True, keep=True)
        l1 = sch.add("sync", lambda s: s.dma_start(out=r1[:1, :], in_=t["splat_lsc"].ap()),
                     dma=True, keep=True)
        l2 = sch.add("sync", lambda s: s.dma_start(out=am_row, in_=t["splat_lam"].ap()),
                     dma=True, keep=True)
        l3 = sch.add("sync", lambda s: s.dma_start(out=s1[:1, 0:1], in_=t["c_ds"].ap()),
                     dma=True, keep=True)
        l4 = sch.add("sync", lambda s: s.dma_start(out=s1[:1, 1:2], in_=t["c_ps"].ap()),
                     dma=True, keep=True)
        l5 = sch.add("sync", lambda s: s.dma_start(out=s1[:1, 2:3], in_=t["c_gsa"].ap()),
                     dma=True, keep=True)

        o_iv = sch.add("scalar", lambda sc: sc.activation(sc_row, r1[:1, :], AF.Exp, scale=-2.0),
                       after=[l1], keep=True)
        o_s = sch.add("scalar", lambda sc: sc.activation(sq[:, :], dire[:, :], AF.Square),
                      after=[ld], keep=True)
        o_m = sch.add("tensor", lambda te: _mm_acc(te, b0[:1, 0:128],
                                                   [(t["ones_col_f32"][:, :1], sq[:, :])]),
                      after=[o_s, C.ops["consts"]], keep=True)
        o_sq = sch.add("scalar", lambda sc: sc.activation(r1[:1, :], pp[:1, 0:128], AF.Sqrt),
                       after=[o_m], keep=True)
        o_ad = sch.add("vector", lambda v: v.tensor_scalar(r1[:1, :], r1[:1, :], 1e-8, None,
                                                           ALU.add), after=[o_sq], keep=True)
        o_rc = sch.add("vector", lambda v: v.reciprocal(rn_row, r1[:1, :]),
                       strict=[o_ad], keep=True)
        o_b = sch.add("tensor", lambda te: _mm_acc(te, b0[:, 0:128],
                                                   [(t["ones_row"][:1, :128], rn_row)]),
                      after=[o_rc, o_sq], keep=True)
        o_db = sch.add("vector", lambda v: v.tensor_mul(dire[:, :], dire[:, :], b0[:, 0:128]),
                       after=[o_b], keep=True)
        o_d1 = sch.add("vector", lambda v: v.tensor_scalar(t["spl"][:, 128:256], dire[:, :],
                                                           SQDI, None, ALU.mult),
                       strict=[o_db], keep=True)
        o_d2 = sch.add("vector", lambda v: v.tensor_copy(t["spl"][:, 384:512], dire[:, :]),
                       strict=[o_d1], keep=True)
        o_b2 = sch.add("tensor", lambda te: _mm_acc(te, b1[:, 0:128],
                                                    [(t["ones_row"][:1, :128], sc_row)]),
                       after=[o_iv], keep=True)
        o_ps = sch.add("vector", lambda v: v.tensor_mul(rb[:, :], pos[:, :], b1[:, 0:128]),
                       after=[o_b2, lp], keep=True)
        o_p1 = sch.add("vector", lambda v: v.tensor_scalar(t["spl"][:, 0:128], rb[:, :],
                                                           SQDI, None, ALU.mult),
                       strict=[o_ps], keep=True)
        o_p2 = sch.add("vector", lambda v: v.tensor_copy(t["spl"][:, 256:384], rb[:, :]),
                       strict=[o_p1], keep=True)
        o_s2 = sch.add("scalar", lambda sc: sc.activation(sq[:, :], pos[:, :], AF.Square),
                       after=[lp, o_m, o_db], keep=True)
        o_m2 = sch.add("tensor", lambda te: _mm_acc(te, b0[:1, 128:256],
                                                    [(t["ones_col_f32"][:, :1], sq[:, :])]),
                       after=[o_s2, o_db], keep=True)
        o_ck = sch.add("vector", lambda v: v.tensor_mul(t["ck_row"], pp[:1, 128:256], sc_row),
                       after=[o_m2, o_iv], keep=True)
        o_ck2 = sch.add("vector", lambda v: v.tensor_scalar(t["ck_row"], t["ck_row"], -0.5,
                                                            None, ALU.mult),
                        strict=[o_ck], keep=True)
        o_cq = sch.add("vector", lambda v: v.tensor_add(t["cq_row"], t["ck_row"], am_row),
                       after=[l2], strict=[o_ck2], keep=True)
        o_co1 = sch.add("vector", lambda v: v.tensor_scalar(t["cok_row"], sc_row, -0.5, None,
                                                            ALU.mult), after=[o_iv], keep=True)
        o_co2 = sch.add("vector", lambda v: v.tensor_scalar(t["coq_row"], t["cok_row"],
                                                            float(HD), None, ALU.mult),
                        strict=[o_co1], keep=True)
        o_pd = sch.add("vector", lambda v: v.tensor_mul(sq[:, :], pos[:, :], dire[:, :]),
                       after=[o_s2, o_m2], strict=[o_db, o_d2], keep=True)
        o_pm = sch.add("tensor", lambda te: _mm_acc(te, b0[:1, 256:384],
                                                    [(t["ones_col_f32"][:, :1], sq[:, :])]),
                       after=[o_pd, o_ck], keep=True)
        o_pd2 = sch.add("vector", lambda v: v.tensor_scalar(t["pd_row"], pp[:1, 256:384],
                                                            -1.0, None, ALU.mult),
                        after=[o_pm], keep=True)
        o_a1 = sch.add("scalar", lambda sc: sc.activation(s1[:1, 3:4], s1[:1, 0:1], AF.Abs),
                       after=[l3], keep=True)
        o_a2 = sch.add("scalar", lambda sc: sc.activation(s1[:1, 4:5], s1[:1, 1:2], AF.Abs),
                       after=[l4], keep=True)
        o_a3 = sch.add("vector", lambda v: v.tensor_scalar(s1[:1, 3:4], s1[:1, 3:4], 1.0,
                                                           None, ALU.add),
                       after=[o_a1], keep=True)
        o_a4 = sch.add("vector", lambda v: v.tensor_mul(s1[:1, 5:6], s1[:1, 3:4], s1[:1, 4:5]),
                       after=[o_a2], strict=[o_a3], keep=True)
        o_a5 = sch.add("vector", lambda v: v.tensor_scalar(s1[:1, 5:6], s1[:1, 5:6], -1.0,
                                                           None, ALU.mult),
                       strict=[o_a4], keep=True)
        o_pp = sch.add("vector", lambda v: v.tensor_mul(s1[:1, 6:7], s1[:1, 0:1], s1[:1, 1:2]),
                       after=[l3, l4], keep=True)
        o_bl = sch.add("scalar", lambda sc: sc.activation(s1[:1, 7:8], s1[:1, 2:3], AF.Sigmoid),
                       after=[l5], keep=True)
        o_b3 = sch.add("vector", lambda v: v.tensor_scalar(s1[:1, 7:8], s1[:1, 7:8], 0.15,
                                                           None, ALU.mult),
                       after=[o_bl], keep=True)
        o_b4 = sch.add("vector", lambda v: v.tensor_scalar(s1[:1, 7:8], s1[:1, 7:8], 0.03,
                                                           None, ALU.min),
                       strict=[o_b3], keep=True)
        o_b5 = sch.add("vector", lambda v: v.tensor_scalar(s1[:1, 8:9], s1[:1, 7:8], -1.0, 1.0,
                                                           ALU.mult, op1=ALU.add),
                       strict=[o_b4], keep=True)
        named = [(1, 0), (6, 1), (7, 2), (8, 3), (5, 4)]
        prev = [o_a5, o_pp, o_b4, o_b5, o_ps]
        last = None
        for si, ci in named:
            ob = sch.add("tensor", lambda te, si=si: _mm_acc(
                te, b1[:, 128:129], [(t["ones_row"][:1, :128], s1[:1, si:si + 1])]),
                after=prev, keep=True)
            oc = sch.add("vector", lambda v, ci=ci: v.tensor_copy(
                cols[:, ci:ci + 1], b1[:, 128:129]), after=[ob], keep=True)
            prev = [oc]
            last = oc
        C.ops["splat_ready"] = sch.add("vector", lambda v: v.engine_nop(),
                                       strict=[last, o_p2, o_d2, o_cq, o_co2,
                                               o_pd2], keep=True)
        sch.flush()


def _attn_iters(hf):
    out = []
    j0 = 4 * hf
    for kc in range(16 * (hf + 1)):
        qoff = max(0, (kc // 4 - j0)) * 128
        out.append((kc, qoff))
    return out


def _attn_heads(C, layer):
    nc, t, sch = C.nc, C.t, C.sch
    dual = layer == 1
    ITORS = {0: _attn_iters(0), 1: _attn_iters(1)}
    qdone = C.ops["qdone"]
    splat = C.ops.get("splat_ready") if dual else None
    selD = C.ops.get("selD_ready") if dual else None

    R = {}

    def rd(key):
        return R.get(("r", key))

    def wr(key):
        return R.get(("w", key))

    def setr(key, op):
        R[("r", key)] = op

    def setw(key, op):
        R[("w", key)] = op

    from contextlib import ExitStack
    with ExitStack() as _es:
        Ksb = _es.enter_context(nc.sbuf_tensor(f"Ksb{layer}", [128, 2 * 32 * 128], BF16))
        Vsb = _es.enter_context(nc.sbuf_tensor(f"Vsb{layer}", [128, 2 * 32 * 128], BF16))
        Pt = _es.enter_context(nc.sbuf_tensor(f"Pt{layer}", [128, (8 if dual else 4) * 512], BF16))
        sqh = _es.enter_context(nc.sbuf_tensor(f"sqh{layer}", [128, 2 * 512], BF16))
        kap = _es.enter_context(nc.sbuf_tensor(f"kap{layer}", [1, 4], F32))
        negm = _es.enter_context(nc.sbuf_tensor(f"negm{layer}", [1, TC], BF16))
        qsqr = _es.enter_context(nc.sbuf_tensor(f"qsqr{layer}", [1, TC], BF16))
        ksqr = _es.enter_context(nc.sbuf_tensor(f"ksqr{layer}", [1, 512], BF16))
        rrow = _es.enter_context(nc.sbuf_tensor(f"rrow{layer}", [33, 512], BF16))
        of32 = _es.enter_context(nc.sbuf_tensor(f"of32{layer}", [128, 2 * 512], BF16))
        if dual:
            negm2 = _es.enter_context(nc.sbuf_tensor("negm2", [1, TC], BF16))
            KB = nc.alloc_sbuf_tensor_at("KB", [8, 32 * 128], BF16, offset=t["_a16"])
            KD = nc.alloc_sbuf_tensor_at("KD", [8, 32 * 128], BF16,
                                         offset=t["_a16"] + 8192)
            QA = _es.enter_context(nc.sbuf_tensor("QA", [8, 512], BF16))
            QC = _es.enter_context(nc.sbuf_tensor("QC", [8, 512], BF16))
        pp = _es.enter_context(nc.psum_tensor(f"PPA{layer}", [128, 8 * 512], F32))
        if dual:
            nS, nG, bOs, bOg, bD, bR = (0, 1), (2, 3), 4, 5, 6, 7
        else:
            nS, bOs, bD, bR = (0, 1, 2), 3, 4, 5

        def bank(i, off=0, hi=512):
            return pp[:, i * 512 + off: i * 512 + hi]

        def prow(i, p0, p1, off=0, hi=512):
            return pp[p0:p1, i * 512 + off: i * 512 + hi]

        def issue_loads(h):
            hb = (h % 2) * 32 * 128
            ok = None
            for r in range(4):
                ok = sch.add("sync", lambda s, h=h, r=r, hb=hb: s.dma_start(
                    out=Ksb[:, hb:hb + 32 * 128]
                    .rearrange("p (u f c) -> p u f c", f=4, c=128)[:, :, r, :],
                    in_=t["k_ga"].ap()[h, r].rearrange("d (u c) -> d u c", c=128)),
                    after=[C.ops[f"ccK{h}"], rd(("Ksb", h % 2))], dma=True, keep=True)
            ov = None
            for r in range(4):
                ov = sch.add("sync", lambda s, h=h, r=r, hb=hb: s.dma_start(
                    out=Vsb[:, hb:hb + 32 * 128]
                    .rearrange("p (u f c) -> p u f c", f=4, c=128)[:, :, r, :],
                    in_=t["v_ga"].ap()[h, r].rearrange("(u p) d -> p u d", p=128)),
                    after=[C.ops[f"ccV{h}"], rd(("Vsb", h % 2))], dma=True, keep=True)
            setw(("Ksb", h % 2), ok)
            setw(("Vsb", h % 2), ov)
            return ok, ov

        for h in range(H):
            ckh = sch.add("gpsimd", lambda g, h=h: g.collective_compute(
                "AllGather", ALU.bypass,
                replica_groups=[[0, 1, 2, 3], [4, 5, 6, 7]],
                ins=[t["k_in"].ap()[h].opt()],
                outs=[t["k_ga"].ap()[h].rearrange("r d n -> (r d) n").opt()]),
                after=[C.ops[f"kst{h}"]], keep=True)
            C.ops[f"ccK{h}"] = ckh
            cvh = sch.add("gpsimd", lambda g, h=h: g.collective_compute(
                "AllGather", ALU.bypass,
                replica_groups=[[0, 1, 2, 3], [4, 5, 6, 7]],
                ins=[t["v_in"].ap()[h].opt()],
                outs=[t["v_ga"].ap()[h].rearrange("r n d -> (r n) d").opt()]),
                after=[C.ops[f"vst{h}"]], keep=True)
            C.ops[f"ccV{h}"] = cvh
        C.ops["ccK"] = C.ops[f"ccK{H - 1}"]
        C.ops["ccV"] = C.ops[f"ccV{H - 1}"]

        lds = {}
        for h in range(H):
            hq = h * TC
            hb = (h % 2) * 32 * 128
            if h == 0:
                lds[0] = issue_loads(0)
            if h + 1 < H:
                lds[h + 1] = issue_loads(h + 1)
            kld, vld = lds[h]

            for kt in range(8):
                sq_o = sch.add("scalar", lambda sc, kt=kt, hb=hb: sc.activation(
                    sqh[:, (kt % 2) * 512:(kt % 2) * 512 + 512],
                    Ksb[:, hb + kt * 512: hb + (kt + 1) * 512], AF.Square),
                    after=[kld, rd(("sqh", kt % 2))], keep=True)
                setr(("Ksb", h % 2), sq_o)
                rmm = sch.add("tensor", lambda te, kt=kt: _mm_acc(
                    te, prow(bR, 0, 1),
                    [(t["ones_col_bf"][:, :1], sqh[:, (kt % 2) * 512:(kt % 2) * 512 + 512])]),
                    after=[sq_o, rd(bR), C.ops["consts"]], keep=True)
                setw(bR, rmm)
                setr(("sqh", kt % 2), rmm)

                if kt == 0:
                    r1o = sch.add("vector", lambda v: v.reduce_max(
                        kap[:1, 0:1], prow(bR, 0, 1), axis=AX.X),
                        after=[rmm, rd("kap")], keep=True)
                else:
                    r1o = sch.add("vector", lambda v: v.reduce_max(
                        kap[:1, 1:2], prow(bR, 0, 1), axis=AX.X),
                        after=[rmm, rd("kap")], keep=True)
                    r1o = sch.add("vector", lambda v: v.tensor_max(
                        kap[:1, 0:1], kap[:1, 0:1], kap[:1, 1:2]),
                        strict=[r1o], keep=True)
                red = sch.add("vector", lambda v: v.tensor_copy(
                    ksqr[:1, :], prow(bR, 0, 1)),
                    after=[rmm, rd("ksqr")], strict=[r1o], keep=True)
                setr(bR, red)
                setw("ksqr", red)
                setw("kap", r1o)

                if dual:
                    def fm1(te, kt=kt, h=h, hb=hb):
                        te.matmul(prow(bR, 0, 8), t["spl"][:, 256 + h * 8:256 + h * 8 + 8],
                                  Ksb[:, hb + kt * 512: hb + (kt + 1) * 512],
                                  start=True, stop=False)
                        te.matmul(prow(bR, 0, 8), t["cok_row"][:1, h * 8:h * 8 + 8],
                                  ksqr[:1, :], start=False, stop=False)
                        return te.matmul(prow(bR, 0, 8), t["ck_row"][:1, h * 8:h * 8 + 8],
                                         t["onesb"][:1, :512], start=False, stop=True)
                    f1 = sch.add("tensor", fm1, after=[red, splat], keep=True)
                    setw(bR, f1)
                    setr("ksqr", f1)
                    eb = sch.add("scalar", lambda sc, kt=kt: sc.activation(
                        KB[:8, kt * 512:(kt + 1) * 512], prow(bR, 0, 8), AF.Exp),
                        after=[f1, rd(("KB", kt))], keep=True)
                    setr(bR, eb)

                    def fm2(te, kt=kt, h=h, hb=hb):
                        te.matmul(prow(bR, 0, 8), t["spl"][:, 384 + h * 8:384 + h * 8 + 8],
                                  Ksb[:, hb + kt * 512: hb + (kt + 1) * 512],
                                  start=True, stop=False)
                        return te.matmul(prow(bR, 0, 8), t["pd_row"][:1, h * 8:h * 8 + 8],
                                         t["onesb"][:1, :512], start=False, stop=True)
                    f2 = sch.add("tensor", fm2, after=[eb], keep=True)
                    setw(bR, f2)
                    setr(("Ksb", h % 2), f2)
                    ed = sch.add("scalar", lambda sc, kt=kt: sc.activation(
                        KD[:8, kt * 512:(kt + 1) * 512], prow(bR, 0, 8), AF.Tanh),
                        after=[f2, rd(("KD", kt))], keep=True)
                    setr(bR, ed)
                    kd2 = sch.add("vector", lambda v, kt=kt: v.tensor_mul(
                        KD[:8, kt * 512:(kt + 1) * 512], KD[:8, kt * 512:(kt + 1) * 512],
                        KB[:8, kt * 512:(kt + 1) * 512]), after=[ed, eb], keep=True)
                    setw(("KB", kt), kd2)
                    setw(("KD", kt), kd2)

            qc_ops = []
            for tt in range(2):
                sq_q = sch.add("scalar", lambda sc, tt=tt, hq=hq: sc.activation(
                    sqh[:, tt * 512: tt * 512 + 512],
                    t["Qh"][:, hq + tt * 512: hq + tt * 512 + 512], AF.Square),
                    after=[qdone, rd(("sqh", tt))], keep=True)
                qm = sch.add("tensor", lambda te, tt=tt: _mm_acc(
                    te, prow(bR, 0, 1),
                    [(t["ones_col_bf"][:, :1], sqh[:, tt * 512: tt * 512 + 512])]),
                    after=[sq_q, rd(bR)], keep=True)
                setw(bR, qm)
                setr(("sqh", tt), qm)
                qc = sch.add("vector", lambda v, tt=tt: v.tensor_copy(
                    qsqr[:1, tt * 512:(tt + 1) * 512], prow(bR, 0, 1)),
                    after=[qm, rd("qsqr")], keep=True)
                setr(bR, qc)
                setw("qsqr", qc)
                qc_ops.append(qc)
            nmul = sch.add("vector", lambda v: v.tensor_scalar(
                negm[:1, :], qsqr[:1, :], kap[:1, 0:1], None, ALU.mult),
                after=[rd("negm")], strict=qc_ops, keep=True)
            setr("kap", nmul)
            sqop = sch.add("scalar", lambda sc: sc.activation(negm[:1, :], negm[:1, :],
                                                              AF.Sqrt),
                           after=[nmul], keep=True)
            negfin = sch.add("vector", lambda v: v.tensor_scalar(
                negm[:1, :], negm[:1, :], -1.02, -0.5, ALU.mult, op1=ALU.add),
                after=[sqop], keep=True)
            setw("negm", negfin)

            for hf in (0, 1):
                its = ITORS[hf]
                nit = len(its)
                q0 = hq + hf * 512

                qa_ready = None
                if dual:
                    def fq1(te, h=h, q0=q0, hq=hq):
                        te.matmul(prow(bR, 0, 8), t["spl"][:, h * 8:h * 8 + 8],
                                  t["Qh"][:, q0:q0 + 512], start=True, stop=False)
                        te.matmul(prow(bR, 0, 8), t["coq_row"][:1, h * 8:h * 8 + 8],
                                  qsqr[:1, (q0 - hq):(q0 - hq) + 512],
                                  start=False, stop=False)
                        return te.matmul(prow(bR, 0, 8), t["cq_row"][:1, h * 8:h * 8 + 8],
                                         t["onesb"][:1, :512], start=False, stop=True)
                    fa = sch.add("tensor", fq1, after=[negfin, rd(bR), splat], keep=True)
                    setw(bR, fa)
                    setr("qsqr", fa)
                    ea = sch.add("scalar", lambda sc: sc.activation(
                        QA[:8, :], prow(bR, 0, 8), AF.Exp),
                        after=[fa, rd("QA")], keep=True)
                    setr(bR, ea)
                    sA = sch.add("tensor", lambda te: _mm_acc(
                        te, prow(bR, 0, 1), [(t["ones_col_bf"][:8, :1], QA[:8, :])]),
                        after=[ea, rd(bR)], keep=True)
                    setw(bR, sA)
                    nm2 = sch.add("vector", lambda v, hf=hf: v.scalar_tensor_tensor(
                        negm2[:1, hf * 512:(hf + 1) * 512], prow(bR, 0, 1),
                        t["spl_cols"][0:1, 4:5], negm[:1, hf * 512:(hf + 1) * 512],
                        ALU.mult, ALU.add), after=[sA, rd("negm2")],
                        strict=[negfin], keep=True)
                    setr(bR, nm2)
                    setw("negm2", nm2)

                    def fq2(te, h=h, q0=q0):
                        te.matmul(prow(bR, 0, 8), t["spl"][:, 128 + h * 8:128 + h * 8 + 8],
                                  t["Qh"][:, q0:q0 + 512], start=True, stop=False)
                        return te.matmul(prow(bR, 0, 8), t["pd_row"][:1, h * 8:h * 8 + 8],
                                         t["onesb"][:1, :512], start=False, stop=True)
                    fb = sch.add("tensor", fq2, after=[nm2], keep=True)
                    setw(bR, fb)
                    ec = sch.add("scalar", lambda sc: sc.activation(
                        QC[:8, :], prow(bR, 0, 8), AF.Tanh),
                        after=[fb, rd("QC")], keep=True)
                    setr(bR, ec)
                    qc1 = sch.add("vector", lambda v: v.tensor_mul(QC[:8, :], QC[:8, :],
                                                                   QA[:8, :]),
                                  after=[ec, ea], keep=True)
                    qa1 = sch.add("vector", lambda v: v.tensor_scalar(
                        QA[:8, :], QA[:8, :], t["spl_cols"][0:8, 0:1], None, ALU.mult),
                        strict=[qc1], keep=True)
                    qa2 = sch.add("vector", lambda v: v.tensor_scalar(
                        QC[:8, :], QC[:8, :], t["spl_cols"][0:8, 1:2], None, ALU.mult),
                        strict=[qa1], keep=True)
                    qa_ready = sch.add("vector", lambda v: v.engine_nop(),
                                       strict=[qa2], keep=True)
                    setw("QA", qa_ready)
                    setw("QC", qa_ready)

                pend_av = []

                def issue_av(fn, deps, psl):
                    oav = sch.add("tensor", fn, after=deps, keep=True)
                    setr(("Pt", psl), oav)
                    if dual:
                        setr(("Pt", 4 + psl), oav)
                    setw(bOs, oav)
                    setw(bD, oav)
                    if dual:
                        setw(bOg, oav)
                    setr(("Ksb", h % 2), oav)
                    setr(("Vsb", h % 2), oav)
                    return oav

                for i, (kc, qoff) in enumerate(its):
                    ssl = i % len(nS)
                    gsl = i % 2
                    psl = i % 4

                    def smm(te, kc=kc, qoff=qoff, ssl=ssl, q0=q0, hf=hf, hb=hb):
                        te.matmul(bank(nS[ssl], qoff), Ksb[:, hb + kc * 128: hb + (kc + 1) * 128],
                                  t["Qh"][:, q0 + qoff: q0 + 512], start=True, stop=False)
                        return te.matmul(bank(nS[ssl], qoff), t["onesb"][:1, :128],
                                         negm[:1, hf * 512 + qoff:(hf + 1) * 512],
                                         start=False, stop=True)
                    om = sch.add("tensor", smm,
                                 after=[negfin, rd(("S", ssl)), kld, C.ops["consts"]],
                                 keep=True)
                    setw(("S", ssl), om)
                    setr("negm", om)
                    oe = sch.add("scalar", lambda sc, ssl=ssl, psl=psl, qoff=qoff: sc.activation(
                        Pt[:, psl * 512 + qoff: psl * 512 + 512], bank(nS[ssl], qoff), AF.Exp),
                        after=[om, rd(("Pt", psl))], keep=True)
                    setr(("S", ssl), oe)
                    if kc // 4 >= 4 * hf:
                        op_ = sch.add("vector", lambda v, psl=psl, qoff=qoff, m=kc % 4:
                                      v.tensor_mul(
                            Pt[:, psl * 512 + qoff: psl * 512 + qoff + 128],
                            Pt[:, psl * 512 + qoff: psl * 512 + qoff + 128],
                            t["mask_sb"][:, m * 128:(m + 1) * 128]),
                            after=[oe, C.ops["ld_mask_sb"]], keep=True)
                    else:
                        op_ = oe

                    if dual:
                        def gmm(te, kc=kc, qoff=qoff, gsl=gsl, q0=q0, hf=hf, hb=hb):
                            te.matmul(bank(nG[gsl], qoff), Ksb[:, hb + kc * 128: hb + (kc + 1) * 128],
                                      t["Qh"][:, q0 + qoff: q0 + 512], start=True, stop=False)
                            te.matmul(bank(nG[gsl], qoff), KB[:8, kc * 128:(kc + 1) * 128],
                                      QA[:8, qoff:512], start=False, stop=False)
                            te.matmul(bank(nG[gsl], qoff), KD[:8, kc * 128:(kc + 1) * 128],
                                      QC[:8, qoff:512], start=False, stop=False)
                            return te.matmul(bank(nG[gsl], qoff), t["onesb"][:1, :128],
                                             negm2[:1, hf * 512 + qoff:(hf + 1) * 512],
                                             start=False, stop=True)
                        og = sch.add("tensor", gmm,
                                     after=[qa_ready, rd(("G", gsl)),
                                            wr(("KB", kc // 4)), wr(("KD", kc // 4))],
                                     keep=True)
                        setw(("G", gsl), og)
                        setr("negm2", og)
                        oge = sch.add("scalar", lambda sc, gsl=gsl, psl=psl, qoff=qoff:
                                      sc.activation(
                            Pt[:, (4 + psl) * 512 + qoff: (4 + psl) * 512 + 512],
                            bank(nG[gsl], qoff), AF.Exp),
                            after=[og, rd(("Pt", 4 + psl))], keep=True)
                        setr(("G", gsl), oge)
                        if kc // 4 >= 4 * hf:
                            ogp = sch.add("vector", lambda v, psl=psl, qoff=qoff, m=kc % 4:
                                          v.tensor_mul(
                                Pt[:, (4 + psl) * 512 + qoff: (4 + psl) * 512 + qoff + 128],
                                Pt[:, (4 + psl) * 512 + qoff: (4 + psl) * 512 + qoff + 128],
                                t["mask_sb"][:, m * 128:(m + 1) * 128]),
                                after=[oge], keep=True)
                        else:
                            ogp = oge

                    def av(te, kc=kc, qoff=qoff, psl=psl, i=i, last=(i == nit - 1), dual=dual,
                           hb=hb):
                        te.matmul(bank(bOs, qoff), Vsb[:, hb + kc * 128: hb + (kc + 1) * 128],
                                  Pt[:, psl * 512 + qoff: psl * 512 + 512],
                                  start=(i == 0), stop=False)
                        if dual:
                            te.matmul(bank(bOg, qoff), Vsb[:, hb + kc * 128: hb + (kc + 1) * 128],
                                      Pt[:, (4 + psl) * 512 + qoff: (4 + psl) * 512 + 512],
                                      start=(i == 0), stop=False)
                            te.matmul(prow(bD, 0, 33, qoff), t["selD"][:, 0:33],
                                      Pt[:, psl * 512 + qoff: psl * 512 + 512],
                                      start=(i == 0), stop=False)
                            return te.matmul(prow(bD, 0, 33, qoff), t["selD"][:, 33:66],
                                             Pt[:, (4 + psl) * 512 + qoff: (4 + psl) * 512 + 512],
                                             start=False, stop=last)
                        return te.matmul(prow(bD, 0, 1, qoff), t["ones_col_bf"][:, :1],
                                         Pt[:, psl * 512 + qoff: psl * 512 + 512],
                                         start=(i == 0), stop=last)
                    deps = [op_, vld]
                    if dual:
                        deps += [ogp, selD]
                    if i == 0:
                        deps += [rd(bOs), rd(bD)]
                        if dual:
                            deps.append(rd(bOg))
                    pend_av.append((av, deps, psl))
                    if len(pend_av) > 2:
                        issue_av(*pend_av.pop(0))
                    setw(("Pt", psl), oe)
                    if dual:
                        setw(("Pt", 4 + psl), oge)

                for pa in pend_av:
                    avl = issue_av(*pa)
                def rc1f(v):
                    with nc.allow_low_precision(reason="softmax scale rows in bf16"):
                        return v.reciprocal(rrow[0:1, :], prow(bD, 0, 1))
                rc1 = sch.add("vector", rc1f, after=[avl], keep=True)
                setr(bD, rc1)
                oev = sch.add("scalar", lambda sc: sc.activation(of32[:, 0:512], bank(bOs),
                                                                 AF.Copy),
                              after=[avl, rd("of32a")], keep=True)
                setr(bOs, oev)
                if dual:
                    def rc2f(v):
                        with nc.allow_low_precision(reason="softmax scale rows in bf16"):
                            return v.reciprocal(rrow[32:33, :], prow(bD, 32, 33))
                    rc2 = sch.add("vector", rc2f, after=[avl], keep=True)
                    setr(bD, rc2)
                    oev2 = sch.add("scalar", lambda sc: sc.activation(of32[:, 512:1024],
                                                                      bank(bOg), AF.Copy),
                                   after=[avl, rd("of32b")], keep=True)
                    setr(bOg, oev2)
                    rb1 = sch.add("tensor", lambda te: _mm_acc(
                        te, bank(bR), [(t["onesb"][:1, :128], rrow[0:1, :])]),
                        after=[rc1, rd(bR)], keep=True)
                    setw(bR, rb1)
                    on1 = sch.add("vector", lambda v: v.scalar_tensor_tensor(
                        of32[:, 0:512], of32[:, 0:512], t["spl_cols"][0:128, 3:4],
                        bank(bR), ALU.mult, ALU.mult), after=[rb1, oev], keep=True)
                    setr(bR, on1)
                    rb2 = sch.add("tensor", lambda te: _mm_acc(
                        te, bank(bR), [(t["onesb"][32:33, :128], rrow[32:33, :])]),
                        after=[rc2, on1], keep=True)
                    setw(bR, rb2)
                    on2 = sch.add("vector", lambda v: v.scalar_tensor_tensor(
                        of32[:, 512:1024], of32[:, 512:1024], t["spl_cols"][0:128, 2:3],
                        bank(bR), ALU.mult, ALU.mult), after=[rb2, oev2], keep=True)
                    setr(bR, on2)
                    onorm = sch.add("vector", lambda v, q0=q0: v.tensor_add(
                        t["Xb"][:, q0:q0 + 512], of32[:, 0:512], of32[:, 512:1024]),
                        strict=[on1, on2], keep=True)
                    setw("of32a", onorm)
                    setw("of32b", onorm)
                else:
                    rb1 = sch.add("tensor", lambda te: _mm_acc(
                        te, bank(bR), [(t["onesb"][:1, :128], rrow[0:1, :])]),
                        after=[rc1, rd(bR)], keep=True)
                    setw(bR, rb1)
                    onorm = sch.add("vector", lambda v, q0=q0: v.tensor_mul(
                        t["Xb"][:, q0:q0 + 512], of32[:, 0:512], bank(bR)),
                        after=[rb1, oev], keep=True)
                    setr(bR, onorm)
                    setw("of32a", onorm)
        C.ops["attn_done"] = onorm
        C.ops["kload_last"] = wr(("Ksb", (H - 1) % 2))
        C.ops["vload_last"] = wr(("Vsb", (H - 1) % 2))
        sch.flush()


def _out_proj(C, layer):
    nc, t, sch = C.nc, C.t, C.sch
    w = t["l0_woT"] if layer == 0 else t["l1_woT"]
    mm = {}
    ev = {}
    attn_done = C.ops["attn_done"]

    with nc.sbuf_tensor(f"wro{layer}", [128, 2 * NB * 128], BF16) as wring, \
            nc.sbuf_tensor(f"bor{layer}", [1, E], BF16) as bor, \
            nc.psum_tensor(f"ppo{layer}", [128, 4 * 512], F32) as pp:
        bias_row = bor if layer == 0 else None
        obo = sch.add("gpsimd", lambda s: s.dma_start(out=bor[:1, :], in_=t["l0_bo_row"].ap()),
                      dma=True, keep=True) if layer == 0 else None
        for fc in range(NB):
            slot = fc % 2
            ld = sch.add("gpsimd", lambda g, fc=fc, slot=slot: g.dma_start(
                out=wring[:, slot * NB * 128:(slot + 1) * NB * 128],
                in_=w.ap()[fc]),
                after=[mm.get((fc - 2, 1))], dma=True, keep=True)
            for tt in range(2):
                bank = pp[:, (slot * 2 + tt) * 512:(slot * 2 + tt + 1) * 512]

                def mmfn(te, bank=bank, slot=slot, tt=tt, fc=fc):
                    pairs = [(wring[:, slot * NB * 128 + ec * 128: slot * NB * 128 + (ec + 1) * 128],
                              t["Xb"][:, ec * TC + tt * 512: ec * TC + (tt + 1) * 512])
                             for ec in range(NB)]
                    tail = ((bias_row[:1, fc * 128:(fc + 1) * 128], t["onesb"][:1, :512])
                            if bias_row is not None else None)
                    return _mm_acc(te, bank, pairs, tail=tail)

                mm[(fc, tt)] = sch.add("tensor", mmfn,
                                       after=[ld, ev.get((fc - 2, tt)), attn_done, obo],
                                       keep=True)
                ev[(fc, tt)] = sch.add("vector", lambda v, bank=bank, fc=fc, tt=tt:
                                       v.tensor_add(
                    t["X"][:, fc * TC + tt * 512: fc * TC + (tt + 1) * 512],
                    t["X"][:, fc * TC + tt * 512: fc * TC + (tt + 1) * 512], bank),
                    after=[mm[(fc, tt)]], keep=True)
        sch.flush()
    C.ops["resid_done"] = ev[(NB - 1, 1)]
    C.ops["resid_mm_last"] = mm[(NB - 1, 1)]


_LN_N = [0]


def _ln(C):
    nc, t, sch = C.nc, C.t, C.sch
    rd_ = C.ops["resid_done"]
    rm = C.ops["resid_mm_last"]
    _LN_N[0] += 1
    li = _LN_N[0]

    with nc.sbuf_tensor(f"lnsq{li}", [128, 2 * 512], BF16) as sq, \
            nc.sbuf_tensor(f"lnrows{li}", [1, 3 * TC], F32) as rows, \
            nc.sbuf_tensor(f"lnmb{li}", [128, 2 * TC], F32) as mb, \
            nc.psum_tensor(f"ppln{li}", [128, 6 * 512], F32) as pp:
        mrow = rows[:1, 0:TC]
        vrow = rows[:1, TC:2 * TC]
        rrow_ln = rows[:1, 2 * TC:3 * TC]

        cast = {}
        for ec in range(NB):
            cast[ec] = sch.add("vector", lambda v, ec=ec: v.tensor_copy(
                t["Xb"][:, ec * TC:(ec + 1) * TC], t["X"][:, ec * TC:(ec + 1) * TC]),
                after=[rd_, rm], keep=True)

        sqo = {}
        mmo = {}
        n = 0
        for ec in range(NB):
            for tt in range(2):
                sqo[n] = sch.add("scalar", lambda sc, ec=ec, tt=tt, n=n: sc.activation(
                    sq[:, (n % 2) * 512:(n % 2) * 512 + 512],
                    t["Xb"][:, ec * TC + tt * 512: ec * TC + tt * 512 + 512], AF.Square),
                    after=[cast[ec], mmo.get(n - 2)], keep=True)

                def mmfn(te, ec=ec, tt=tt, n=n):
                    te.matmul(pp[:1, tt * 512:(tt + 1) * 512], t["mean_col_bf"][:, :1],
                              t["Xb"][:, ec * TC + tt * 512: ec * TC + tt * 512 + 512],
                              start=(ec == 0), stop=(ec == NB - 1))
                    return te.matmul(pp[:1, 1024 + tt * 512: 1024 + (tt + 1) * 512],
                                     t["mean_col_bf"][:, :1],
                                     sq[:, (n % 2) * 512:(n % 2) * 512 + 512],
                                     start=(ec == 0), stop=(ec == NB - 1))
                mmo[n] = sch.add("tensor", mmfn,
                                 after=[sqo[n], cast[ec], C.ops["consts"]], keep=True)
                n += 1

        stats = [mmo[n - 1], mmo[n - 2]]
        cm = sch.add("vector", lambda v: v.tensor_copy(mrow, pp[:1, 0:1024]),
                     after=stats, keep=True)
        vm = sch.add("vector", lambda v: v.tensor_mul(vrow, mrow, mrow),
                     strict=[cm], keep=True)
        vs = sch.add("vector", lambda v: v.scalar_tensor_tensor(
            vrow, pp[:1, 1024:2048], LN_EPS, vrow, ALU.add, ALU.subtract),
            after=stats, strict=[vm], keep=True)
        sr = sch.add("scalar", lambda sc: sc.activation(vrow, vrow, AF.Sqrt),
                     after=[vs], keep=True)
        rr = sch.add("vector", lambda v: v.reciprocal(rrow_ln, vrow), after=[sr], keep=True)
        bc = {}
        for tt in range(2):
            b1 = sch.add("tensor", lambda te, tt=tt: _mm_acc(
                te, pp[:, 2048 + tt * 512: 2048 + (tt + 1) * 512],
                [(t["ones_row"][:1, :128], mrow[:1, tt * 512:(tt + 1) * 512])]),
                after=[rr, C.ops["consts"]], keep=True)
            c1 = sch.add("scalar", lambda sc, tt=tt: sc.activation(
                mb[:, tt * 512:(tt + 1) * 512], pp[:, 2048 + tt * 512: 2048 + (tt + 1) * 512],
                AF.Copy), after=[b1], keep=True)
            b2 = sch.add("tensor", lambda te, tt=tt: _mm_acc(
                te, pp[:, 2048 + tt * 512: 2048 + (tt + 1) * 512],
                [(t["ones_row"][:1, :128], rrow_ln[:1, tt * 512:(tt + 1) * 512])]),
                after=[c1], keep=True)
            c2 = sch.add("scalar", lambda sc, tt=tt: sc.activation(
                mb[:, TC + tt * 512: TC + (tt + 1) * 512],
                pp[:, 2048 + tt * 512: 2048 + (tt + 1) * 512], AF.Copy),
                after=[b2], keep=True)
            bc[tt] = c2

        fin = None
        for ec in range(NB):
            for tt in range(2):
                def nfn(v, ec=ec, tt=tt):
                    xsl = t["X"][:, ec * TC + tt * 512: ec * TC + (tt + 1) * 512]
                    v.tensor_sub(xsl, xsl, mb[:, tt * 512:(tt + 1) * 512])
                    v.tensor_mul(xsl, xsl, mb[:, TC + tt * 512: TC + (tt + 1) * 512])
                    return v.tensor_copy(
                        t["Xb"][:, ec * TC + tt * 512: ec * TC + (tt + 1) * 512], xsl)
                fin = sch.add("vector", nfn,
                              after=[bc[tt], mmo[2 * ec + tt], sqo[2 * ec + tt]], keep=True)
        sch.flush()
    C.ops["xb_ready"] = fin
    C.ops["ln_done"] = fin



def _moe_ffn(C):
    nc, t, sch = C.nc, C.t, C.sch
    ln_done = C.ops["ln_done"]
    if C.dbg_on and "dbgsnap" not in t:
        t["dbgsnap"] = nc.alloc_sbuf_tensor("dbgsnap", [128, 24], F32)

    with nc.sbuf_tensor("sel64", [128, 64 * 128], BF16) as sel64, \
            nc.sbuf_tensor("io64", [128, 128], F32) as io64, \
            nc.sbuf_tensor("wT_sb", [128, 128], BF16) as wT_sb, \
            nc.sbuf_tensor("wr_sb", [128, NB * 8], BF16) as wr_sb, \
            nc.sbuf_tensor("probs", [128, 128], F32) as probs, \
            nc.sbuf_tensor("prow", [128, 16], F32) as prow, \
            nc.sbuf_tensor("wtm", [128, 128], F32) as wtm, \
            nc.sbuf_tensor("wtr", [128, 128], F32) as wtr, \
            nc.sbuf_tensor("wbe", [128, TC], BF16) as wbe, \
            nc.sbuf_tensor("gsb", [128, NFC * TC], BF16) as gsb, \
            nc.sbuf_tensor("wrg", [128, 2 * NB * 128], BF16) as wrg, \
            nc.sbuf_tensor("wrd", [128, 2 * NFC * 128], BF16) as wrd, \
            nc.psum_tensor("ppm", [128, 8 * 512], F32) as pp:
        lw = sch.add("gpsimd", lambda g: g.dma_start(
            out=wr_sb[:, :], in_=t["l0_wrT"].ap()),
            after=[ln_done], dma=True, keep=True)
        cp = {}
        for tc in range(8):
            bsl = pp[:, (4 + tc % 2) * 512: (4 + tc % 2) * 512 + 8]

            def rmm(te, tc=tc, bsl=bsl):
                pairs = [(t["Xb"][:, ec * TC + tc * 128: ec * TC + (tc + 1) * 128],
                          wr_sb[:, ec * 8:(ec + 1) * 8]) for ec in range(NB)]
                return _mm_acc(te, bsl, pairs)
            om = sch.add("tensor", rmm, after=[lw, ln_done, cp.get(tc - 2)], keep=True)
            cp[tc] = sch.add("vector", lambda v, tc=tc, bsl=bsl: v.tensor_copy(
                probs[:, tc * 8:(tc + 1) * 8], bsl), after=[om], keep=True)

        def v3(ap):
            return ap.rearrange("p (a b) -> p a b", b=8)

        def vb(ap):
            return ap.rearrange("p (a b) -> p a b", b=1).broadcast_to([128, 8, 8])

        r8a = prow[:, 0:8]
        r8b = prow[:, 8:16]
        lg = probs[:, 0:64]
        pb = probs[:, 64:128]
        o1 = sch.add("vector", lambda v: v.tensor_reduce(
            r8a, v3(lg), AX.X, ALU.max), after=[cp[7]], strict=[cp[7]], keep=True)
        o2 = sch.add("vector", lambda v: v.tensor_tensor(
            v3(lg), v3(lg), vb(r8a), ALU.subtract), strict=[o1], keep=True)
        o3 = sch.add("scalar", lambda sc: sc.activation(
            pb, lg, AF.Exp), after=[o2], keep=True)
        o4 = sch.add("vector", lambda v: v.tensor_reduce(
            r8a, v3(pb), AX.X, ALU.add), after=[o3], keep=True)
        o5 = sch.add("vector", lambda v: v.reciprocal(r8b, r8a), strict=[o4], keep=True)
        o6 = sch.add("vector", lambda v: v.tensor_tensor(
            v3(pb), v3(pb), vb(r8b), ALU.mult), strict=[o5], keep=True)
        o7 = sch.add("vector", lambda v: v.tensor_reduce(
            r8a, v3(pb), AX.X, ALU.max), strict=[o6], keep=True)
        o8 = sch.add("vector", lambda v: v.tensor_tensor(
            v3(wtm[:, 0:64]), v3(pb), vb(r8a), ALU.is_ge), strict=[o7], keep=True)
        o9 = sch.add("vector", lambda v: v.scalar_tensor_tensor(
            wtm[:, 64:128], wtm[:, 0:64], -1e30, pb, ALU.mult, ALU.add),
            strict=[o8], keep=True)
        oA = sch.add("vector", lambda v: v.tensor_reduce(
            r8b, v3(wtm[:, 64:128]), AX.X, ALU.max), strict=[o9], keep=True)
        oB = sch.add("vector", lambda v: v.tensor_tensor(
            v3(wtm[:, 0:64]), v3(pb), vb(r8b), ALU.is_ge), strict=[oA], keep=True)
        oC0 = sch.add("vector", lambda v: v.tensor_tensor(
            wtm[:, 0:64], wtm[:, 0:64], pb, ALU.mult), strict=[oB], keep=True)
        if C.dbg_on:
            sch.add("vector", lambda v: v.tensor_copy(
                t["dbgsnap"][:, 0:8], pb[:, 0:8]), strict=[oC0], keep=True)
            sch.add("vector", lambda v: v.tensor_copy(
                t["dbgsnap"][:, 16:24], wtm[:, 0:8]), strict=[oC0], keep=True)
        oP = sch.add("vector", lambda v: v.tensor_copy(
            wtr[:, 0:64].rearrange("p (e u) -> p e u", u=8),
            wtm[:, 0:64].rearrange("p (u e) -> p e u", e=8)), strict=[oC0], keep=True)
        oZ = sch.add("vector", lambda v: v.memset(wtr[:, 64:128], 0.0),
                     strict=[oP], keep=True)
        oT = sch.add("tensor", lambda te: te.transpose(
            pp[:, 6 * 512: 6 * 512 + 128], wtr[:, :], t["ident32"][:, :]),
            after=[oP, oZ, C.ops["ident_ready"]], keep=True)
        oS = sch.add("vector", lambda v: v.tensor_copy(
            wT_sb[:, :], pp[:, 6 * 512: 6 * 512 + 128]), after=[oT], keep=True)
        oi = sch.add("gpsimd", lambda g: g.iota(io64[:, :], [[0, 128]], channel_multiplier=1,
                                                allow_small_or_imprecise_dtypes=True),
                     keep=True)
        sel_ready = None
        for c in range(64):
            sel_ready = sch.add("vector", lambda v, c=c: v.tensor_scalar(
                sel64[:, c * 128:(c + 1) * 128], io64[:, :], float(c), None, ALU.is_equal),
                after=[oi], keep=True)
        router_done = oS
        if getattr(C, "router_only", False):
            sch.flush()
            C.ops["resid_done"] = oS
            C.ops["resid_mm_last"] = oT
            return
        vexp = [("r", e) for e in range(MOE_E)] + [("s", 0), ("s", 1)]
        dmm_last_prev = router_done
        dev_last_prev = None
        wbe_used_prev = None
        gsb_free_prev = router_done
        sel_prev = router_done
        for vi, (kind, arg) in enumerate(vexp):
            routed = kind == "r"
            if routed:
                e = arg

                def smf(te, e=e):
                    last = None
                    for tc in range(8):
                        c = e * 8 + tc
                        bb = 6 if tc < 4 else 7
                        last = te.matmul(
                            pp[:, bb * 512 + (tc % 4) * 128: bb * 512 + (tc % 4) * 128 + 128],
                            sel64[:, c * 128:(c + 1) * 128], wT_sb[:, :],
                            start=(tc % 4 == 0), stop=(tc % 4 == 3))
                    return last
                sm1 = sch.add("tensor", smf,
                              after=[router_done, wbe_used_prev, sel_prev, sel_ready,
                                     dmm_last_prev, dev_last_prev], keep=True)
                wb1 = sch.add("vector", lambda v: v.tensor_copy(
                    wbe[:, 0:512], pp[:, 6 * 512:7 * 512]), after=[sm1], keep=True)
                wb2 = sch.add("vector", lambda v: v.tensor_copy(
                    wbe[:, 512:1024], pp[:, 7 * 512:8 * 512]), after=[sm1], keep=True)
                wbe_ready = wb2
                sel_prev = wb2
                wg = t["l0_wgT"].ap()[e]
                wu = t["l0_wuT"].ap()[e]
                wd = t["l0_wdT"].ap()[e]
            else:
                wbe_ready = None
                hc = arg * NFC
                h0 = arg * MOE_F
                wg = t["l0_wsgT"].ap()[hc:hc + NFC]
                wu = t["l0_wsuT"].ap()[hc:hc + NFC]
                wd = t["l0_wsdT"].ap()[:, :, h0:h0 + MOE_F]

            gmm = {}
            gev = {}
            for fc in range(NFC):
                war = gmm.get((fc - 2, 1)) or C.ops.get(f"umm_{vi - 1}_last")
                ldg = sch.add("gpsimd", lambda g, fc=fc, wg=wg: g.dma_start(
                    out=wrg[:, (fc % 2) * NB * 128:(fc % 2 + 1) * NB * 128],
                    in_=wg[fc]),
                    after=[war], dma=True, keep=True)
                for tt in range(2):
                    bk = pp[:, ((fc % 2) * 2 + tt) * 512:((fc % 2) * 2 + tt + 1) * 512]

                    def gf(te, fc=fc, tt=tt, bk=bk):
                        pairs = [(wrg[:, (fc % 2) * NB * 128 + ec * 128:(fc % 2) * NB * 128 + (ec + 1) * 128],
                                  t["Xb"][:, ec * TC + tt * 512: ec * TC + (tt + 1) * 512])
                                 for ec in range(NB)]
                        return _mm_acc(te, bk, pairs)
                    om = sch.add("tensor", gf,
                                 after=[ldg, gev.get((fc - 2, tt)), dmm_last_prev], keep=True)
                    gmm[(fc, tt)] = om
                    gev[(fc, tt)] = sch.add("scalar", lambda sc, fc=fc, tt=tt, bk=bk:
                                            sc.activation(
                        gsb[:, fc * TC + tt * 512: fc * TC + (tt + 1) * 512], bk, AF.Silu),
                        after=[om, gsb_free_prev], keep=True)
            umm = {}
            uev = {}
            for fc in range(NFC):
                war = umm.get((fc - 2, 1)) or gmm[(NFC - 1, 1)]
                ldu = sch.add("gpsimd", lambda g, fc=fc, wu=wu: g.dma_start(
                    out=wrg[:, (fc % 2) * NB * 128:(fc % 2 + 1) * NB * 128],
                    in_=wu[fc]),
                    after=[war], dma=True, keep=True)
                for tt in range(2):
                    bk = pp[:, ((fc % 2) * 2 + tt) * 512:((fc % 2) * 2 + tt + 1) * 512]

                    def uf(te, fc=fc, tt=tt, bk=bk):
                        pairs = [(wrg[:, (fc % 2) * NB * 128 + ec * 128:(fc % 2) * NB * 128 + (ec + 1) * 128],
                                  t["Xb"][:, ec * TC + tt * 512: ec * TC + (tt + 1) * 512])
                                 for ec in range(NB)]
                        return _mm_acc(te, bk, pairs)
                    om = sch.add("tensor", uf,
                                 after=[ldu, uev.get((fc - 2, tt)), gev[(fc, tt)]], keep=True)
                    umm[(fc, tt)] = om

                    def hf(v, fc=fc, tt=tt, bk=bk, routed=routed):
                        gs = gsb[:, fc * TC + tt * 512: fc * TC + (tt + 1) * 512]
                        if routed:
                            v.tensor_mul(gs, gs, bk)
                            return v.tensor_mul(gs, gs, wbe[:, tt * 512:(tt + 1) * 512])
                        return v.tensor_mul(gs, gs, bk)
                    dep = [om]
                    if routed:
                        dep.append(wbe_ready)
                    uev[(fc, tt)] = sch.add("vector", hf, after=dep, keep=True)
            hready = uev[(NFC - 1, 1)]
            if vi == 0:
                C.dbg("wbe0", wbe[:, :], (128, TC), [wbe_ready])
                C.dbg("h0", gsb[:, 0:TC], (128, TC), [hready])
            wbe_used_prev = hready
            C.ops[f"umm_{vi}_last"] = umm[(NFC - 1, 1)]

            dmm = {}
            dev = {}
            for dc in range(NB):
                war = dmm.get((dc - 2, 1)) or C.ops.get(f"dmm_{vi - 1}_last")
                ldd = sch.add("gpsimd", lambda g, dc=dc, wd=wd: g.dma_start(
                    out=wrd[:, (dc % 2) * NFC * 128:(dc % 2 + 1) * NFC * 128],
                    in_=wd[dc]),
                    after=[war], dma=True, keep=True)
                for tt in range(2):
                    bk = pp[:, (4 + (dc % 2) * 2 + tt) * 512:(4 + (dc % 2) * 2 + tt + 1) * 512]

                    def df(te, dc=dc, tt=tt, bk=bk):
                        pairs = [(wrd[:, (dc % 2) * NFC * 128 + f * 128:(dc % 2) * NFC * 128 + (f + 1) * 128],
                                  gsb[:, f * TC + tt * 512: f * TC + (tt + 1) * 512])
                                 for f in range(NFC)]
                        return _mm_acc(te, bk, pairs)
                    om = sch.add("tensor", df,
                                 after=[ldd, dev.get((dc - 2, tt)), hready], keep=True)
                    dmm[(dc, tt)] = om
                    dev[(dc, tt)] = sch.add("vector", lambda v, dc=dc, tt=tt, bk=bk:
                                            v.tensor_add(
                        t["X"][:, dc * TC + tt * 512: dc * TC + (tt + 1) * 512],
                        t["X"][:, dc * TC + tt * 512: dc * TC + (tt + 1) * 512], bk),
                        after=[om], keep=True)
            dmm_last_prev = dmm[(NB - 1, 1)]
            C.ops[f"dmm_{vi}_last"] = dmm_last_prev
            gsb_free_prev = dmm_last_prev
            dev_last = dev[(NB - 1, 1)]
            dev_last_prev = dev_last
        C.ops["resid_done"] = dev_last
        C.ops["resid_mm_last"] = dmm_last_prev
        sch.flush()



def _quant_ffn(C):
    nc, t, sch = C.nc, C.t, C.sch
    ln_done = C.ops["ln_done"]

    hgrp = nc.alloc_sbuf_tensor_at("hgrp", [128, 16 * TC], BF16, offset=t["_aQh"])
    with nc.sbuf_tensor("qsc", [128, 144], F32) as qsc, \
            nc.sbuf_tensor("xsum", [1, TC], BF16) as xsum, \
            nc.sbuf_tensor("hsum", [1, TC], BF16) as hsum, \
            nc.sbuf_tensor("gslot", [128, 2 * TC], BF16) as gslot, \
            nc.sbuf_tensor("wrq", [128, 2 * NB * 128], BF16) as wrq, \
            nc.sbuf_tensor("wrdq", [128, 2 * 16 * 128], BF16) as wrdq, \
            nc.psum_tensor("ppq", [128, 8 * 512], F32) as pp:
        l1 = sch.add("sync", lambda s: s.dma_start(
            out=qsc[:, 0:64], in_=t["l1_gsc"].ap()), dma=True, keep=True)
        l2 = sch.add("sync", lambda s: s.dma_start(
            out=qsc[:, 64:128], in_=t["l1_usc"].ap()), dma=True, keep=True)
        l3 = sch.add("sync", lambda s: s.dma_start(
            out=qsc[:, 128:144], in_=t["l1_dsc"].ap()), dma=True, keep=True)

        xs_prev = None
        xs = {}
        for tt in range(2):
            om = sch.add("tensor", lambda te, tt=tt: _mm_acc(
                te, pp[:1, 7 * 512: 7 * 512 + 512],
                [(t["ones_col_bf"][:, :1],
                  t["Xb"][:, ec * TC + tt * 512: ec * TC + (tt + 1) * 512])
                 for ec in range(NB)]),
                after=[ln_done, xs_prev, C.ops["consts"]], keep=True)
            xs[tt] = sch.add("vector", lambda v, tt=tt: v.tensor_copy(
                xsum[:1, tt * 512:(tt + 1) * 512], pp[:1, 7 * 512: 7 * 512 + 512]),
                after=[om], keep=True)
            xs_prev = xs[tt]
        xsum_done = xs[1]

        dmm_last_prev = None
        hs_prev = xsum_done
        for g in range(4):
            gmmset = {}
            gev = {}
            umm = {}
            uev = {}
            for fc in range(16):
                fi = g * 16 + fc
                war = umm.get((fc - 2, 1)) or dmm_last_prev
                lg = sch.add("gpsimd", lambda gg, fi=fi, fc=fc: gg.dma_start(
                    out=wrq[:, (fc % 2) * NB * 128:(fc % 2 + 1) * NB * 128],
                    in_=t["l1_gwT"].ap()[fi]),
                    after=[war], dma=True, keep=True)
                for tt in range(2):
                    bk = pp[:, ((fc % 2) * 2 + tt) * 512:((fc % 2) * 2 + tt + 1) * 512]

                    def gf(te, fc=fc, tt=tt, bk=bk):
                        pairs = [(wrq[:, (fc % 2) * NB * 128 + ec * 128:(fc % 2) * NB * 128 + (ec + 1) * 128],
                                  t["Xb"][:, ec * TC + tt * 512: ec * TC + (tt + 1) * 512])
                                 for ec in range(NB)]
                        tail = (t["neg128_row"][:1, :128], xsum[:1, tt * 512:(tt + 1) * 512])
                        return _mm_acc(te, bk, pairs, tail=tail)
                    om = sch.add("tensor", gf,
                                 after=[lg, xsum_done, gev.get((fc - 2, tt))], keep=True)
                    gmmset[(fc, tt)] = om
                    gev[(fc, tt)] = sch.add("scalar", lambda sc, fi=fi, fc=fc, tt=tt, bk=bk:
                                            sc.activation(
                        gslot[:, (fc % 2) * TC + tt * 512:(fc % 2) * TC + tt * 512 + 512],
                        bk, AF.Silu, scale=qsc[:, fi:fi + 1]),
                        after=[om, uev.get((fc - 2, tt)), l1], keep=True)
                lu = sch.add("gpsimd", lambda gg, fi=fi, fc=fc: gg.dma_start(
                    out=wrq[:, (fc % 2) * NB * 128:(fc % 2 + 1) * NB * 128],
                    in_=t["l1_gwT" if False else "l1_uwT"].ap()[fi]),
                    after=[gmmset[(fc, 1)]], dma=True, keep=True)
                for tt in range(2):
                    bk = pp[:, (4 + (fc % 2) * 2 + tt) * 512:(4 + (fc % 2) * 2 + tt + 1) * 512]

                    def uf(te, fc=fc, tt=tt, bk=bk):
                        pairs = [(wrq[:, (fc % 2) * NB * 128 + ec * 128:(fc % 2) * NB * 128 + (ec + 1) * 128],
                                  t["Xb"][:, ec * TC + tt * 512: ec * TC + (tt + 1) * 512])
                                 for ec in range(NB)]
                        tail = (t["neg128_row"][:1, :128], xsum[:1, tt * 512:(tt + 1) * 512])
                        return _mm_acc(te, bk, pairs, tail=tail)
                    war_u = uev.get((fc - 2, tt)) or C.ops.get(f"qdev_{g - 1}")
                    om = sch.add("tensor", uf, after=[lu, war_u], keep=True)
                    umm[(fc, tt)] = om
                    uev[(fc, tt)] = sch.add("vector", lambda v, fi=fi, fc=fc, tt=tt, bk=bk:
                                            v.scalar_tensor_tensor(
                        hgrp[:, fc * TC + tt * 512: fc * TC + (tt + 1) * 512],
                        bk, qsc[:, 64 + fi:64 + fi + 1],
                        gslot[:, (fc % 2) * TC + tt * 512:(fc % 2) * TC + tt * 512 + 512],
                        ALU.mult, ALU.mult),
                        after=[om, gev[(fc, tt)], hgrp_free(C, g, fc, tt), l2], keep=True)
                    C.ops[f"qh_{g}_{fc}_{tt}"] = uev[(fc, tt)]
            hready = uev[(15, 1)]
            C.ops[f"qu_{g}_last"] = umm[(15, 1)]

            hs = {}
            for tt in range(2):
                om = sch.add("tensor", lambda te, tt=tt: _mm_acc(
                    te, pp[:1, tt * 512:(tt + 1) * 512],
                    [(t["ones_col_bf"][:, :1],
                      hgrp[:, f * TC + tt * 512: f * TC + (tt + 1) * 512])
                     for f in range(16)]), after=[hready, hs_prev], keep=True)
                hs[tt] = sch.add("vector", lambda v, tt=tt: v.tensor_copy(
                    hsum[:1, tt * 512:(tt + 1) * 512], pp[:1, tt * 512:(tt + 1) * 512]),
                    after=[om], keep=True)
            hs_prev = hs[1]

            dmm = {}
            dev = {}
            for dc in range(NB):
                war = dmm.get((dc - 2, 1))
                ldw = sch.add("gpsimd", lambda gg, g=g, dc=dc: gg.dma_start(
                    out=wrdq[:, (dc % 2) * 16 * 128:(dc % 2 + 1) * 16 * 128],
                    in_=t["l1_dwT"].ap()[dc, :, g * 16 * 128:(g + 1) * 16 * 128]),
                    after=[war or dmm_last_prev], dma=True, keep=True)
                for tt in range(2):
                    bk = pp[:, (4 + (dc % 2) * 2 + tt) * 512:(4 + (dc % 2) * 2 + tt + 1) * 512]

                    def df(te, dc=dc, tt=tt, bk=bk):
                        pairs = [(wrdq[:, (dc % 2) * 16 * 128 + f * 128:(dc % 2) * 16 * 128 + (f + 1) * 128],
                                  hgrp[:, f * TC + tt * 512: f * TC + (tt + 1) * 512])
                                 for f in range(16)]
                        tail = (t["neg128_row"][:1, :128], hsum[:1, tt * 512:(tt + 1) * 512])
                        return _mm_acc(te, bk, pairs, tail=tail)
                    om = sch.add("tensor", df,
                                 after=[ldw, dev.get((dc - 2, tt)), hs[tt]], keep=True)
                    dmm[(dc, tt)] = om
                    dev[(dc, tt)] = sch.add("vector", lambda v, dc=dc, tt=tt, bk=bk:
                                            v.scalar_tensor_tensor(
                        t["X"][:, dc * TC + tt * 512: dc * TC + (tt + 1) * 512],
                        bk, qsc[:, 128 + dc:128 + dc + 1],
                        t["X"][:, dc * TC + tt * 512: dc * TC + (tt + 1) * 512],
                        ALU.mult, ALU.add),
                        after=[om, l3], keep=True)
            dmm_last_prev = dmm[(NB - 1, 1)]
            C.ops[f"qd_{g}_last"] = dmm_last_prev
            dev_last = dev[(NB - 1, 1)]
            C.ops[f"qdev_{g}"] = dev_last
        C.ops["resid_done"] = dev_last
        C.ops["resid_mm_last"] = dmm_last_prev
        sch.flush()


def hgrp_free(C, g, fc, tt):
    if g == 0:
        return None
    return C.ops.get(f"qd_{g - 1}_last")



def _tap(C, tag):
    t, sch = C.t, C.sch
    o = sch.add("sync", lambda s: s.dma_start(
        out=t["tap_" + tag].ap().rearrange("(e p) n -> p e n", p=128),
        in_=t["X"][:, :].rearrange("p (e n) -> p e n", e=NB)),
        after=[C.ops.get("ln_done")], dma=True, keep=True)
    sch.add("vector", lambda v: v.engine_nop(), after=[o], keep=True)
    sch.flush()


def _store_out(C):
    t, sch = C.t, C.sch
    o = sch.add("sync", lambda s: s.dma_start(
        out=t["y"].ap(), in_=t["X"][:, :]),
        after=[C.ops.get("ln_done"), C.ops.get("resid_done")], dma=True, keep=True)
    sch.add("vector", lambda v: v.engine_nop(), after=[o], keep=True)
    sch.flush()



def kernel(**inputs):
    in_maps = _prep_inputs(inputs)
    nc = build_nc(taps=_TAPS, upto=_UPTO)
    res = run_bass_kernel_spmd(nc, in_maps, core_ids=list(range(8)))
    out = np.zeros((B, S, E), dtype=np.float32)
    for cid in range(8):
        b, cc = cid // 4, cid % 4
        rows = _core_rows(cc)
        yc = np.asarray(res.results[cid]["y"]).reshape(128, NB, TC)
        out[b][rows] = yc.transpose(1, 0, 2).reshape(E, TC).T
    return out
